# version 11
# baseline (speedup 1.0000x reference)
import numpy as np
import ml_dtypes
import concourse.bass as bass
import concourse.mybir as mybir
from concourse.bass_utils import run_bass_kernel_spmd

F32 = mybir.dt.float32
BF16 = mybir.dt.bfloat16
I32 = mybir.dt.int32
AF = mybir.ActivationFunctionType
ALU = mybir.AluOpType
AX = mybir.AxisListType

B = 4
T = 4096
D = 2048
DC = 16
HD = 5632
HC = 44
NS = 16
DTR = 64
TG = 512
NG = 8
OWN_HC = 8
NQ = 4
EPS = 1e-5
INV_SQRT_HD = float(1.0 / np.sqrt(128.0))

BF = ml_dtypes.bfloat16


class Sem:
    def __init__(self, nc, name):
        self.h = nc.alloc_semaphore(name)
        self.n = 0

    def inc(self, instr, amt=1):
        instr.then_inc(self.h, amt)
        self.n += amt
        return self.n


def build_program(dbg=False, do_conv=True, ng=NG, do_rest=True, nk=4, sub=(2, 3, 4), dbg_ph=0, no_poll=False, att_zero=False, no_pois=False, trunc=4):
    nc = bass.Bass("TRN2", target_bir_lowering=False, debug=False,
                   detect_race_conditions=False)

    x_in = nc.dram_tensor("x_in", [D, T], F32, kind="ExternalInput").ap()
    wg_i = nc.dram_tensor("wg_i", [D, HD], I32, kind="ExternalInput").ap()
    wu_i = nc.dram_tensor("wu_i", [D, HD], I32, kind="ExternalInput").ap()
    wd_i = nc.dram_tensor("wd_i", [HD, D], I32, kind="ExternalInput").ap()
    wdtin = nc.dram_tensor("wdtin", [D, DTR], F32, kind="ExternalInput").ap()
    wdtout = nc.dram_tensor("wdtout", [DTR, 1024], F32, kind="ExternalInput").ap()
    wbc = nc.dram_tensor("wbc", [D, 2 * NS], F32, kind="ExternalInput").ap()
    alog = nc.dram_tensor("alog", [128, OWN_HC * NS], F32, kind="ExternalInput").ap()
    dvec = nc.dram_tensor("dvec", [128, OWN_HC], F32, kind="ExternalInput").ap()
    bdt = nc.dram_tensor("bdt", [128, OWN_HC], F32, kind="ExternalInput").ap()
    sg_in = nc.dram_tensor("sg_in", [128, HC], F32, kind="ExternalInput").ap()
    su_in = nc.dram_tensor("su_in", [128, HC], F32, kind="ExternalInput").ap()
    sd_in = nc.dram_tensor("sd_in", [128, DC], F32, kind="ExternalInput").ap()
    ident_in = nc.dram_tensor("ident_in", [128, 128], BF16, kind="ExternalInput").ap()
    indic_in = nc.dram_tensor("indic_in", [32, 32 * 128], BF16, kind="ExternalInput").ap()
    ones_b_in = nc.dram_tensor("ones_b_in", [128, 1], BF16, kind="ExternalInput").ap()
    onesr_f_in = nc.dram_tensor("onesr_f_in", [1, 128], F32, kind="ExternalInput").ap()
    onesr_b_in = nc.dram_tensor("onesr_b_in", [1, 128], BF16, kind="ExternalInput").ap()
    negr_b_in = nc.dram_tensor("negr_b_in", [1, 128], BF16, kind="ExternalInput").ap()
    nones_b_in = nc.dram_tensor("nones_b_in", [128, 1], BF16, kind="ExternalInput").ap()
    role_in = nc.dram_tensor("role_in", [1, 1], I32, kind="ExternalInput").ap()
    c128_in = nc.dram_tensor("c128_in", [128, 2], F32, kind="ExternalInput").ap()
    c1_in = nc.dram_tensor("c1_in", [1, 8], F32, kind="ExternalInput").ap()

    y_out = nc.dram_tensor("y_out", [D, 4 * TG], F32, kind="ExternalOutput").ap()
    if dbg:
        dbg_x = nc.dram_tensor("dbg_x", [128, DC, TG], F32, kind="ExternalOutput").ap()
        dbg_xn = nc.dram_tensor("dbg_xn", [128, DC, TG], F32, kind="ExternalOutput").ap()
        dbg_dt = nc.dram_tensor("dbg_dt", [128, OWN_HC, TG], F32, kind="ExternalOutput").ap()
        dbg_dtu = nc.dram_tensor("dbg_dtu", [128, OWN_HC, TG], F32, kind="ExternalOutput").ap()
        dbg_bb = nc.dram_tensor("dbg_bb", [128, NS, TG], F32, kind="ExternalOutput").ap()
        dbg_cb = nc.dram_tensor("dbg_cb", [128, NS, TG], F32, kind="ExternalOutput").ap()
        dbg_sh = nc.dram_tensor("dbg_sh", [128, NS, TG], F32, kind="ExternalOutput").ap()
        dbg_sc = nc.dram_tensor("dbg_sc", [128, NS, TG], F32, kind="ExternalOutput").ap()
        dbg_st8 = nc.dram_tensor("dbg_st8", [128, OWN_HC, NS], F32, kind="ExternalOutput").ap()
        dbg_bc = nc.dram_tensor("dbg_bc", [32, TG], F32, kind="ExternalOutput").ap()
        dbg_dtr = nc.dram_tensor("dbg_dtr", [DTR, TG], F32, kind="ExternalOutput").ap()
        dbg_asb = nc.dram_tensor("dbg_asb", [128, OWN_HC * NS], F32, kind="ExternalOutput").ap()
        dbg_x2 = nc.dram_tensor("dbg_x2", [128, DC, TG], F32, kind="ExternalOutput").ap()
        dbg_fl = nc.dram_tensor("dbg_fl", [4, 16], I32, kind="ExternalOutput").ap()
        dbg_wg = nc.dram_tensor("dbg_wg", [128, DC, 128], BF16, kind="ExternalOutput").ap()
        dbg_wd = nc.dram_tensor("dbg_wd", [128, 512], BF16, kind="ExternalOutput").ap()
        dbg_z = nc.dram_tensor("dbg_z", [128, HC, TG], F32, kind="ExternalOutput").ap()
        dbg_att = nc.dram_tensor("dbg_att", [128, 2, 256], F32, kind="ExternalOutput").ap()
        dbg_attb = nc.dram_tensor("dbg_attb", [128, 2, 256], F32, kind="ExternalOutput").ap()
        dbg_gtb = nc.dram_tensor("dbg_gtb", [128, 6, 128], F32, kind="ExternalOutput").ap()
        dbg_pb = nc.dram_tensor("dbg_pb", [128, 256], F32, kind="ExternalOutput").ap()
        dbg_cm = nc.dram_tensor("dbg_cm", [128, 16], F32, kind="ExternalOutput").ap()
        dbg_pp = nc.dram_tensor("dbg_pp", [4, 4], F32, kind="ExternalOutput").ap()

    wg_s = nc.dram_tensor("wg_s", [HC, 128, DC, 128], BF16, kind="Internal").ap()
    wu_s = nc.dram_tensor("wu_s", [HC, 128, DC, 128], BF16, kind="Internal").ap()
    wd_s = nc.dram_tensor("wd_s", [NQ, HC, 128, 512], BF16, kind="Internal").ap()
    xbuf = nc.dram_tensor("xbuf", [4, DC, 128, TG], F32, kind="Internal").ap()
    shx = nc.dram_tensor("shx", [16, OWN_HC, 128, TG], F32, kind="Internal",
                         addr_space="Shared").ap()
    shfl = nc.dram_tensor("shfl", [2, 8], I32, kind="Internal", addr_space="Shared").ap()

    sb = nc.alloc_sbuf_tensor
    x_t = sb("x_t", [128, DC, TG], F32)
    xn_t = sb("xn_t", [128, DC, TG], BF16)
    tscr = sb("tscr", [128, 1, TG], F32)
    xsq = sb("xsq", [128, 2, TG], BF16)
    rows = sb("rows", [1, 8 * TG], F32)
    rows_b = sb("rows_b", [1, 2 * TG], BF16)
    a_t = sb("a_t", [128, 2, TG], F32)
    wdtin_t = sb("wdtin_t", [128, DC, DTR], BF16)
    wdtout_t = sb("wdtout_t", [DTR, OWN_HC, 128], BF16)
    wbc_t = sb("wbc_t", [128, DC, 32], BF16)
    a_sb = sb("a_sb", [128, OWN_HC * NS], F32)
    d_sb = sb("d_sb", [128, OWN_HC], F32)
    bdt_sb = sb("bdt_sb", [128, OWN_HC], F32)
    sg_sb = sb("sg_sb", [128, HC], F32)
    su_sb = sb("su_sb", [128, HC], F32)
    sd_sb = sb("sd_sb", [128, DC], F32)
    ident_t = sb("ident_t", [128, 128], BF16)
    indic_t = sb("indic_t", [32, 32 * 128], BF16)
    ones_b = sb("ones_b", [128, 1], BF16)
    onesr_f = sb("onesr_f", [1, 128], F32)
    onesr_b = sb("onesr_b", [1, 128], BF16)
    negr_b = sb("negr_b", [1, 128], BF16)
    nones_b = sb("nones_b", [128, 1], BF16)
    c128_sb = sb("c128_sb", [128, 2], F32)
    c1_sb = sb("c1_sb", [1, 8], F32)
    role_sb = sb("role_sb", [1, 1], I32)
    flag_sb = sb("flag_sb", [1, 8], I32)
    fpoll_sb = sb("fpoll_sb", [1, 16], I32)
    pois_i = sb("pois_i", [1, 1], I32)
    pois_f = sb("pois_f", [1, 4], F32)

    ps = nc.alloc_psum_tensor
    PA = ps("PA", [128, TG], F32)
    PB = ps("PB", [128, TG], F32)
    PC = ps("PC", [128, TG], F32)
    PD = ps("PD", [128, TG], F32)
    PE_ = ps("PE_", [64, TG], F32)
    PF = ps("PF", [32, TG], F32)
    PT = ps("PT", [128, 768], F32)

    s_ld = Sem(nc, "s_ld")
    s_gd = Sem(nc, "s_gd")
    s_g2 = Sem(nc, "s_g2")
    s_mm = Sem(nc, "s_mm")
    s_sq = Sem(nc, "s_sq")
    s_v = Sem(nc, "s_v")
    s_a = Sem(nc, "s_a")
    s_st = Sem(nc, "s_st")
    s_xc = Sem(nc, "s_xc")
    s_fin = Sem(nc, "s_fin")
    s_w0 = Sem(nc, "s_w0")
    s_w1 = Sem(nc, "s_w1")
    s_wd = Sem(nc, "s_wd")
    s_cv = Sem(nc, "s_cv")
    s_pl = Sem(nc, "s_pl")

    def W(eng, sem, val):
        if val and val > 0:
            eng.wait_ge(sem.h, val)

    GE = nc.gpsimd
    SY = nc.sync
    freg = SY.alloc_register("freg")
    creg = SY.alloc_register("creg")
    pcreg = SY.alloc_register("pcreg")
    pc2reg = SY.alloc_register("pc2reg")

    seq = {}
    vals = {}

    def per_role(emit):
        for arm in nc.switch_core_id([nc.gpsimd], n=128):
            emit(nc.gpsimd, int(arm) % 2)

    with nc.Block() as blk:
        @blk.sync
        def _(sy):
            for dst, src in [(a_sb[:], alog), (d_sb[:], dvec), (bdt_sb[:], bdt),
                             (sg_sb[:], sg_in), (su_sb[:], su_in), (sd_sb[:], sd_in),
                             (ident_t[:], ident_in), (indic_t[:], indic_in),
                             (ones_b[:], ones_b_in), (onesr_f[:], onesr_f_in),
                             (onesr_b[:], onesr_b_in), (negr_b[:], negr_b_in),
                             (nones_b[:], nones_b_in), (c128_sb[:], c128_in),
                             (c1_sb[:], c1_in)]:
                i = sy.dma_start(out=dst, in_=src)
                s_ld.inc(i, 16)
        INIT_LD = s_ld.n

        @blk.gpsimd
        def _(ge):
            i = ge.dma_start(out=wdtin_t[:], in_=wdtin.rearrange("(c p) r -> p c r", p=128))
            s_gd.inc(i, 16)
            i = ge.dma_start(out=wdtout_t[:], in_=wdtout.rearrange("r (c h) -> r c h", h=128))
            s_gd.inc(i, 16)
            i = ge.dma_start(out=wbc_t[:], in_=wbc.rearrange("(c p) n -> p c n", p=128))
            s_gd.inc(i, 16)
            i = ge.dma_start(out=role_sb[:], in_=role_in)
            s_gd.inc(i, 16)
            W(ge, s_gd, s_gd.n)

        @blk.scalar
        def _(se):
            W(se, s_ld, INIT_LD)
            W(se, s_gd, s_gd.n)
            i = se.activation(out=a_sb[:], in_=a_sb[:], func=AF.Exp)
            s_a.inc(i, 1)
        @blk.vector
        def _(ve):
            W(ve, s_a, s_a.n)
            i = ve.tensor_scalar_mul(a_sb[:], a_sb[:], -1.0)
            s_v.inc(i, 1)
        A_READY_V = s_v.n

    wconv_scope = nc.sbuf_tensor("wconv_t", [128, HD], BF16)
    wconv_t = wconv_scope.__enter__()
    if not do_conv:
        CONV_DONE = 0
    if do_conv:
        with nc.Block() as blk:
            @blk.gpsimd
            def _(ge):
                for wi, ws in [(wg_i, wg_s), (wu_i, wu_s)]:
                    for dc in range(DC):
                        W(ge, s_cv, s_cv.n)
                        i = ge.dma_start(out=wconv_t[:], in_=wi[dc * 128:(dc + 1) * 128, :])
                        s_gd.inc(i, 16)
                        W(ge, s_gd, s_gd.n)
                        for part in range(4):
                            h0, h1 = part * 11, (part + 1) * 11
                            dst = ws[h0:h1, :, dc, :]
                            i = ge.dma_start(
                                out=dst.rearrange("c p h -> p c h"),
                                in_=wconv_t[:, h0 * 128:h1 * 128].rearrange("p (c h) -> p c h", h=128))
                            s_cv.inc(i, 16)
                for hk in range(HC):
                    W(ge, s_cv, s_cv.n)
                    i = ge.dma_start(out=wconv_t[:, 0:D], in_=wd_i[hk * 128:(hk + 1) * 128, :])
                    s_gd.inc(i, 16)
                    W(ge, s_gd, s_gd.n)
                    dst = wd_s[:, hk, :, :]
                    i = ge.dma_start(out=dst.rearrange("q p t -> p q t"),
                                     in_=wconv_t[:, 0:D].rearrange("p (q t) -> p q t", t=512))
                    s_cv.inc(i, 16)
                W(ge, s_cv, s_cv.n)

        CONV_DONE = s_cv.n
        if dbg:
            with nc.Block() as blkw:
                @blkw.gpsimd
                def _(ge):
                    W(ge, s_cv, CONV_DONE)
                    i = ge.dma_start(out=dbg_wg, in_=wg_s[3].rearrange("p c h -> p c h"))
                    s_gd.inc(i, 16)
                    i = ge.dma_start(out=dbg_wd, in_=wd_s[1, 5])
                    s_gd.inc(i, 16)
                    W(ge, s_gd, s_gd.n)

    def emit_ln(load_done, blk=None):
        deps = load_done if isinstance(load_done, list) else [load_done]
        blk_cm = None
        if blk is None:
            blk_cm = nc.Block()
            blk = blk_cm.__enter__()

        sq_base = s_sq.n
        act_base = s_a.n

        @blk.scalar
        def _(se):
            for sem_, val_ in deps:
                W(se, sem_, val_)
            W(se, s_mm, seq.get("ln_stats", 0))
            for dc in range(DC):
                if dc >= 2:
                    W(se, s_sq, sq_base + dc - 1)
                i = se.activation(out=xsq[:, dc % 2, :], in_=xn_t[:, dc, :], func=AF.Square)
                s_a.inc(i, 1)

        @blk.tensor
        def _(te):
            for sem_, val_ in deps:
                W(te, sem_, val_)
            W(te, s_v, seq.get("ln_rows", 0))
            W(te, s_v, seq.get("pef_free", 0))
            W(te, s_a, seq.get("pe_evac", 0))
            for dc in range(DC):
                te.matmul(PE_[0:1, :], ones_b[:], xn_t[:, dc, :], start=(dc == 0), stop=(dc == DC - 1))
            for dc in range(DC):
                te.wait_ge(s_a.h, act_base + dc + 1)
                i = te.matmul(PF[0:1, :], ones_b[:], xsq[:, dc % 2, :],
                              start=(dc == 0), stop=(dc == DC - 1))
                if dc == DC - 1:
                    s_mm.inc(i, 1)
                else:
                    s_sq.inc(i, 1)
        stats_done = s_mm.n
        seq["ln_stats"] = stats_done

        @blk.vector
        def _(ve):
            W(ve, s_mm, stats_done)
            ve.tensor_scalar(out=rows[:, 0:TG], in0=PE_[0:1, :], scalar1=1.0 / D,
                             scalar2=None, op0=ALU.mult)
            ve.tensor_scalar(out=rows[:, TG:2 * TG], in0=PF[0:1, :], scalar1=1.0 / D,
                             scalar2=None, op0=ALU.mult)
            ve.tensor_tensor(out=rows[:, 2 * TG:3 * TG], in0=rows[:, 0:TG],
                             in1=rows[:, 0:TG], op=ALU.mult)
            ve.tensor_tensor(out=rows[:, 2 * TG:3 * TG], in0=rows[:, TG:2 * TG],
                             in1=rows[:, 2 * TG:3 * TG], op=ALU.subtract)
            i = ve.tensor_copy(out=rows_b[:, 0:TG], in_=rows[:, 0:TG])
            s_v.inc(i, 1)
        seq["ln_rows"] = s_v.n

        @blk.scalar
        def _(se):
            W(se, s_v, seq["ln_rows"])
            se.activation(out=rows[:, 3 * TG:4 * TG], in_=rows[:, 2 * TG:3 * TG],
                          func=AF.Ln, bias=c1_sb[0:1, 0:1])
            i = se.activation(out=rows[:, 3 * TG:4 * TG], in_=rows[:, 3 * TG:4 * TG],
                              func=AF.Exp, scale=c1_sb[0:1, 2:3])
            s_a.inc(i, 1)
        rstd_done = s_a.n

        @blk.tensor
        def _(te):
            W(te, s_a, rstd_done)
            W(te, s_a, seq.get("pa_free", 0))
            W(te, s_v, seq.get("pb_free", 0))
            te.matmul(PA[:], onesr_f[:], rows[:, 0:TG], start=True, stop=True)
            i = te.matmul(PB[:], onesr_f[:], rows[:, 3 * TG:4 * TG], start=True, stop=True)
            s_mm.inc(i, 1)
        bcast_done = s_mm.n

        @blk.vector
        def _(ve):
            W(ve, s_mm, bcast_done)
            for dc in range(DC):
                ve.tensor_tensor(out=tscr[:, 0, :], in0=x_t[:, dc, :],
                                 in1=PA[:], op=ALU.subtract)
                i = ve.tensor_tensor(out=xn_t[:, dc, :], in0=tscr[:, 0, :],
                                     in1=PB[:], op=ALU.mult)
            s_v.inc(i, 1)
        if blk_cm is not None:
            blk_cm.__exit__(None, None, None)
        return s_v.n

    def emit_recast(blk):
        @blk.vector
        def _(ve):
            W(ve, s_mm, seq.get("xn_pe_free", 0))
            for dc in range(DC):
                i = ve.tensor_copy(out=xn_t[:, dc, :], in_=x_t[:, dc, :])
            s_v.inc(i, 1)
        return (s_v, s_v.n)

    import contextlib
    ph1_stack = contextlib.ExitStack()
    dtr_t = ph1_stack.enter_context(nc.sbuf_tensor("dtr_t", [DTR, TG], BF16))
    bc_t = ph1_stack.enter_context(nc.sbuf_tensor("bc_t", [32, TG], BF16))
    dt_t = ph1_stack.enter_context(nc.sbuf_tensor("dt_t", [128, OWN_HC, TG], BF16))
    dtu_t = ph1_stack.enter_context(nc.sbuf_tensor("dtu_t", [128, OWN_HC, TG], BF16))
    bu_t = ph1_stack.enter_context(nc.sbuf_tensor("bu_t", [128, 1, TG], F32))
    s_hc = ph1_stack.enter_context(nc.sbuf_tensor("s_hc", [128, NS, TG], F32))
    sc_hc = ph1_stack.enter_context(nc.sbuf_tensor("sc_hc", [128, NS, TG], BF16))
    bb_all = ph1_stack.enter_context(nc.sbuf_tensor("bb_all", [128, NS, TG], BF16))
    cb_all = ph1_stack.enter_context(nc.sbuf_tensor("cb_all", [128, NS, TG], BF16))
    yh_t = ph1_stack.enter_context(nc.sbuf_tensor("yh_t", [128, 1, TG], F32))
    state_sb = ph1_stack.enter_context(nc.sbuf_tensor("state_sb", [128, OWN_HC, NS], F32))

    for g in range(ng):
        with nc.Block() as blk:
            @blk.sync
            def _(sy, g=g):
                W(sy, s_gd, seq.get("sh_written", 0))
                xsrc = x_in.rearrange("(c p) t -> p c t", p=128)[:, :, g * TG:(g + 1) * TG]
                i = sy.dma_start(out=x_t[:], in_=xsrc)
                s_ld.inc(i, 16)
            x_ld = s_ld.n

            @blk.vector
            def _(ve, x_ld=x_ld):
                W(ve, s_ld, x_ld)
                W(ve, s_mm, seq.get("xn_pe_free", 0))
                for dc in range(DC):
                    i = ve.tensor_copy(out=xn_t[:, dc, :], in_=x_t[:, dc, :])
                s_v.inc(i, 1)
            XN0 = (s_v, s_v.n)

            XN = emit_ln([XN0, (s_ld, x_ld)], blk=blk)

            @blk.tensor
            def _(te):
                W(te, s_v, XN)
                W(te, s_a, seq.get("pe_evac", 0))
                for dc in range(DC):
                    te.matmul(PE_[:], wdtin_t[:, dc, :], xn_t[:, dc, :],
                              start=(dc == 0), stop=(dc == DC - 1))
                for dc in range(DC):
                    i = te.matmul(PF[:], wbc_t[:, dc, :], xn_t[:, dc, :],
                                  start=(dc == 0), stop=(dc == DC - 1))
                s_mm.inc(i, 1)
            proj_done = s_mm.n
            seq["xn_pe_free"] = proj_done

            @blk.scalar
            def _(se):
                W(se, s_mm, proj_done)
                se.copy(out=dtr_t[:], in_=PE_[:])
                i = se.copy(out=bc_t[:], in_=PF[:])
                s_a.inc(i, 1)
            seq["pe_evac"] = s_a.n
            evac_done = s_a.n

            for hc in range(OWN_HC):
                @blk.tensor
                def _(te, hc=hc):
                    W(te, s_a, evac_done)
                    if hc > 0:
                        W(te, s_a, seq["dt_act"])
                    i = te.matmul(PC[:], wdtout_t[:, hc, :], dtr_t[:], start=True, stop=True)
                    s_mm.inc(i, 1)
                dtmm = s_mm.n

                @blk.scalar
                def _(se, hc=hc, dtmm=dtmm):
                    W(se, s_mm, dtmm)
                    if hc > 0:
                        W(se, s_v, seq["dt_tt"])
                    se.activation(out=a_t[:, 0, :], in_=PC[:], func=AF.Abs,
                                  bias=bdt_sb[:, hc:hc + 1])
                    se.activation(out=a_t[:, 0, :], in_=a_t[:, 0, :], func=AF.Exp, scale=c128_sb[:, 0:1])
                    se.activation(out=a_t[:, 0, :], in_=a_t[:, 0, :], func=AF.Ln, bias=1.0)
                    i = se.activation(out=a_t[:, 1, :], in_=PC[:], func=AF.Relu,
                                      bias=bdt_sb[:, hc:hc + 1])
                    s_a.inc(i, 1)
                seq["dt_act"] = s_a.n

                @blk.vector
                def _(ve, hc=hc):
                    W(ve, s_a, seq["dt_act"])
                    ve.tensor_tensor(out=dt_t[:, hc, :], in0=a_t[:, 0, :],
                                     in1=a_t[:, 1, :], op=ALU.add)
                    i = ve.tensor_tensor(out=dtu_t[:, hc, :], in0=dt_t[:, hc, :],
                                         in1=xn_t[:, hc, :], op=ALU.mult)
                    s_v.inc(i, 1)
                seq["dt_tt"] = s_v.n
            dt_done_v = s_v.n
            dt_done_a = s_a.n

            for n in range(NS):
                pB = PA if n % 2 == 0 else PC
                pC = PB if n % 2 == 0 else PD

                @blk.tensor
                def _(te, n=n, pB=pB, pC=pC):
                    if n == 0:
                        W(te, s_v, dt_done_v)
                    if n == 1:
                        W(te, s_a, dt_done_a)
                    if n >= 2:
                        W(te, s_a, seq[f"bank_free_{n % 2}"])
                    ind_b = indic_t[:, n * 128:(n + 1) * 128]
                    ind_c = indic_t[:, (16 + n) * 128:(17 + n) * 128]
                    te.matmul(pB[:], ind_b, bc_t[:], start=True, stop=True)
                    i = te.matmul(pC[:], ind_c, bc_t[:], start=True, stop=True)
                    s_mm.inc(i, 1)
                bcst = s_mm.n

                @blk.scalar
                def _(se, n=n, g=g, bcst=bcst, pB=pB, pC=pC):
                    W(se, s_mm, bcst)
                    if g > 0 and n == 0:
                        W(se, s_v, seq["scan_prev_tg"])
                    se.copy(out=bb_all[:, n, :], in_=pB[:])
                    i = se.copy(out=cb_all[:, n, :], in_=pC[:])
                    s_a.inc(i, 1)
                seq[f"bank_free_{n % 2}"] = s_a.n
            bc_built = s_a.n

            for hc in range(OWN_HC):
                for n in range(NS):
                    @blk.scalar
                    def _(se, hc=hc, n=n):
                        if hc == 0 and n == 0:
                            W(se, s_v, A_READY_V)
                        if hc == 0 and n < 2:
                            W(se, s_v, dt_done_v)
                        W(se, s_v, seq.get(f"a_used_{n % 2}", 0))
                        idx = hc * NS + n
                        i = se.activation(out=a_t[:, n % 2, :], in_=dt_t[:, hc, :],
                                          func=AF.Exp, scale=a_sb[:, idx:idx + 1])
                        s_a.inc(i, 1)
                    a_rdy = s_a.n

                    @blk.vector
                    def _(ve, hc=hc, n=n, g=g, a_rdy=a_rdy):
                        if hc == 0 and n == 0:
                            W(ve, s_a, bc_built)
                        W(ve, s_a, a_rdy)
                        ve.tensor_tensor(out=bu_t[:, 0, :], in0=dtu_t[:, hc, :],
                                         in1=bb_all[:, n, :], op=ALU.mult)
                        init = state_sb[:, hc, n:n + 1] if g > 0 else 0.0
                        ve.tensor_tensor_scan(out=s_hc[:, n, :], data0=a_t[:, n % 2, :],
                                              data1=bu_t[:, 0, :], initial=init,
                                              op0=ALU.mult, op1=ALU.add)
                        i = ve.tensor_tensor(out=sc_hc[:, n, :], in0=s_hc[:, n, :],
                                             in1=cb_all[:, n, :], op=ALU.mult)
                        s_v.inc(i, 1)
                        seq[f"a_used_{n % 2}"] = s_v.n

                @blk.vector
                def _(ve, hc=hc):
                    ve.tensor_copy(out=state_sb[:, hc, :], in_=s_hc[:, :, TG - 1:TG].squeeze(2))
                    ve.tensor_reduce(out=yh_t[:, 0, :],
                                     in_=sc_hc[:, :, :].rearrange("p n t -> p t n"),
                                     axis=AX.X, op=ALU.add)
                    ve.scalar_tensor_tensor(out=yh_t[:, 0, :], in0=xn_t[:, hc, :],
                                            scalar=d_sb[:, hc:hc + 1], in1=yh_t[:, 0, :],
                                            op0=ALU.mult, op1=ALU.add)
                    i = ve.tensor_tensor(out=x_t[:, hc, :], in0=x_t[:, hc, :],
                                         in1=yh_t[:, 0, :], op=ALU.add)
                    s_v.inc(i, 1)
            seq["scan_prev_tg"] = s_v.n
            y_done = s_v.n

            @blk.gpsimd
            def _(ge, g=g):
                W(ge, s_v, y_done)
        def wr_sh(ge, j, g=g):
            ge.dma_start(out=shx[8 * j + g].rearrange("c p t -> p c t"),
                         in_=x_t[:, 0:OWN_HC, :]).then_inc(s_gd.h, 16)
        per_role(wr_sh)
        s_gd.n += 16
        seq["sh_written"] = s_gd.n
        with nc.Block() as blk2:
            @blk2.gpsimd
            def _(ge, g=g):
                W(ge, s_gd, s_gd.n)
                ge.memset(flag_sb[:], g + 1)
        def wr_fl(ge, j):
            ge.dma_start(out=shfl[j:j + 1, :], in_=flag_sb[:]).then_inc(s_gd.h, 16)
        per_role(wr_fl)
        s_gd.n += 16
        with nc.Block() as blk3:
            @blk3.gpsimd
            def _(ge):
                W(ge, s_gd, s_gd.n)
        if dbg and g == 0:
            with nc.Block() as blkd:
                @blkd.gpsimd
                def _(ge):
                    for dst, srct in [(dbg_x, x_t[:]), (dbg_xn, xn_t[:]),
                                      (dbg_dt, dt_t[:]), (dbg_dtu, dtu_t[:]),
                                      (dbg_bb, bb_all[:]), (dbg_cb, cb_all[:]),
                                      (dbg_sh, s_hc[:]), (dbg_sc, sc_hc[:]),
                                      (dbg_st8, state_sb[:]), (dbg_bc, bc_t[:]),
                                      (dbg_dtr, dtr_t[:]), (dbg_asb, a_sb[:])]:
                        i = ge.dma_start(out=dst, in_=srct)
                        s_gd.inc(i, 16)
                    W(ge, s_gd, s_gd.n)

    PH1_GD = s_gd.n

    if do_rest:
        XCHG = 0

        ph1_stack.close()
        wconv_scope.__exit__(None, None, None)
        z_t = sb("z_t", [128, HC, TG], BF16)
        wgu_t = sb("wgu_t", [128, 2, 2, DC, 128], BF16)
        wdq_t = sb("wdq_t", [128, HC, 512], BF16)
        att_t = sb("att_t", [128, 2, 256], F32)
        attb_t = sb("attb_t", [128, 2, 256], BF16)
        gtb_t = sb("gtb_t", [128, 6, 128], BF16)
        colmax = sb("colmax", [128, 8], F32)
        colsum = sb("colsum", [128, 8], F32)

        def emit_ld_shared(k):
            with nc.Block() as blk0:
                @blk0.gpsimd
                def _(ge):
                    W(ge, s_fin, 16 * k)
                    i = ge.dma_start(out=fpoll_sb[:, 0:16],
                                     in_=shfl.rearrange("a b -> (a b)")[None, :])
                    s_g2.inc(i, 16)
            def rd(ge, j, k=k):
                so = shx[9 * j + 2 * k].rearrange("c p t -> p c t")
                sp = shx[8 - 7 * j + 2 * k].rearrange("c p t -> p c t")
                ge.dma_start(out=x_t[:, 0:OWN_HC, :], in_=so).then_inc(s_g2.h, 16)
                ge.dma_start(out=xn_t[:, 0:OWN_HC, :], in_=so).then_inc(s_g2.h, 16)
                ge.dma_start(out=x_t[:, OWN_HC:DC, :], in_=sp).then_inc(s_g2.h, 16)
                ge.dma_start(out=xn_t[:, OWN_HC:DC, :], in_=sp).then_inc(s_g2.h, 16)
            per_role(rd)
            s_g2.n += 64
            ldv = s_g2.n
            if dbg:
                with nc.Block() as blkf:
                    @blkf.gpsimd
                    def _(ge, k=k):
                        i = ge.dma_start(out=dbg_fl[k:k + 1, :], in_=fpoll_sb[:])
                        s_g2.inc(i, 16)
                ldv2 = s_g2.n
            with nc.Block() as blkp:
                @blkp.vector
                def _(ve, k=k):
                    W(ve, s_g2, ldv)
                    i = ve.tensor_tensor(out=pois_i[:], in0=fpoll_sb[0:1, 0:1],
                                         in1=fpoll_sb[0:1, 8:9], op=ALU.min)
                    s_v.inc(i, 1)
                mn = s_v.n

                @blkp.scalar
                def _(se, k=k, mn=mn):
                    W(se, s_v, mn)
                    i = se.activation(out=pois_f[:, 0:1], in_=pois_i[:], func=AF.Relu,
                                      scale=c1_sb[0:1, 3:4], bias=c1_sb[0:1, 4 + k:5 + k])
                    s_a.inc(i, 1)
                rl = s_a.n

                @blkp.vector
                def _(ve, k=k, rl=rl):
                    W(ve, s_a, rl)
                    i = ve.scalar_tensor_tensor(out=x_t[0:1, 0, 0:1], in0=pois_f[0:1, 0:1],
                                                scalar=0.0 if no_pois else 1.0e30,
                                                in1=x_t[0:1, 0, 0:1],
                                                op0=ALU.mult, op1=ALU.add)
                    s_v.inc(i, 1)
            if dbg:
                with nc.Block() as blkpp:
                    @blkpp.gpsimd
                    def _(ge, k=k):
                        W(ge, s_v, s_v.n)
                        i = ge.dma_start(out=dbg_pp[k:k + 1, 0:4], in_=pois_f[:, 0:4])
                        s_g2.inc(i, 16)
                        W(ge, s_g2, s_g2.n)
            return (s_g2, s_g2.n)

        def emit_ld_xbuf(k, stored_at):
            with nc.Block() as blk0:
                @blk0.gpsimd
                def _(ge):
                    W(ge, s_st, stored_at)
                    xsrc = xbuf[k].rearrange("c p t -> p c t")
                    i = ge.dma_start(out=x_t[:], in_=xsrc)
                    s_g2.inc(i, 16)
                    i = ge.dma_start(out=xn_t[:], in_=xsrc)
                    s_g2.inc(i, 16)
            return (s_g2, s_g2.n)

        def emit_ffn(blk, k, store_to_y, store=None, ln_done=0):
            @blk.tensor
            def _(te, ln_done=ln_done):
                W(te, s_v, max(seq["ln_rows"], ln_done))
                for dc in range(DC):
                    i = te.matmul(PE_[0:1, :], ones_b[:], xn_t[:, dc, :],
                                  start=(dc == 0), stop=(dc == DC - 1))
                s_mm.inc(i, 1)
            cs_done = s_mm.n

            @blk.vector
            def _(ve):
                W(ve, s_mm, cs_done)
                i = ve.tensor_copy(out=rows_b[:, 0:TG], in_=PE_[0:1, :])
                s_v.inc(i, 1)
            csrow_done = s_v.n

            for hc in range(HC):
                buf = hc % 2

                s_wb = s_w0 if buf == 0 else s_w1

                @blk.sync
                def _(sy, hc=hc, buf=buf, s_wb=s_wb):
                    W(sy, s_mm, seq.get(f"gu_mm_{buf}", 0))
                    sy.dma_start(out=wgu_t[:, buf, 0, 0:8, :], in_=wg_s[hc, :, 0:8, :]).then_inc(s_wb.h, 16)
                    sy.dma_start(out=wgu_t[:, buf, 0, 8:16, :], in_=wg_s[hc, :, 8:16, :]).then_inc(s_wb.h, 16)
                    sy.dma_start(out=wgu_t[:, buf, 1, 0:8, :], in_=wu_s[hc, :, 0:8, :]).then_inc(s_wb.h, 16)
                    i = sy.dma_start(out=wgu_t[:, buf, 1, 8:16, :], in_=wu_s[hc, :, 8:16, :])
                    i.then_inc(s_wb.h, 16)
                s_wb.n += 64
                w_ld = s_wb.n

                @blk.tensor
                def _(te, hc=hc, buf=buf, w_ld=w_ld, s_wb=s_wb):
                    W(te, s_wb, w_ld)
                    W(te, s_v, seq.get(f"gu_evac_{buf}", 0))
                    W(te, s_v, csrow_done)
                    W(te, s_a, seq.get("pa_free", 0))
                    W(te, s_v, seq.get("pb_free", 0))
                    for dc in range(DC):
                        te.matmul(PA[:], wgu_t[:, buf, 0, dc, :], xn_t[:, dc, :],
                                  start=(dc == 0), stop=False)
                    te.matmul(PA[:], negr_b[:], rows_b[:, 0:TG], start=False, stop=True)
                    for dc in range(DC):
                        te.matmul(PB[:], wgu_t[:, buf, 1, dc, :], xn_t[:, dc, :],
                                  start=(dc == 0), stop=False)
                    i = te.matmul(PB[:], negr_b[:], rows_b[:, 0:TG], start=False, stop=True)
                    s_mm.inc(i, 1)
                seq[f"gu_mm_{buf}"] = s_mm.n
                gu_mm = s_mm.n

                @blk.scalar
                def _(se, hc=hc, gu_mm=gu_mm):
                    W(se, s_mm, gu_mm)
                    W(se, s_v, seq.get(f"gu_evac_{hc % 2}", 0))
                    i = se.activation(out=a_t[:, hc % 2, :], in_=PA[:], func=AF.Silu,
                                      scale=sg_sb[:, hc:hc + 1])
                    s_a.inc(i, 1)
                silu_done = s_a.n
                seq["pa_free"] = silu_done

                @blk.vector
                def _(ve, hc=hc, silu_done=silu_done):
                    W(ve, s_a, silu_done)
                    i = ve.scalar_tensor_tensor(out=z_t[:, hc, :], in0=PB[:],
                                                scalar=su_sb[:, hc:hc + 1],
                                                in1=a_t[:, hc % 2, :],
                                                op0=ALU.mult, op1=ALU.mult)
                    s_v.inc(i, 1)
                seq[f"gu_evac_{buf}"] = s_v.n
                seq["pb_free"] = s_v.n

                @blk.tensor
                def _(te, hc=hc):
                    W(te, s_v, s_v.n)
                    i = te.matmul(PE_[0:1, :], nones_b[:], z_t[:, hc, :],
                                  start=(hc == 0), stop=(hc == HC - 1))
                    if hc == HC - 1:
                        s_mm.inc(i, 1)
            zs_done = s_mm.n
            seq["xn_pe_free"] = s_mm.n

            @blk.vector
            def _(ve):
                W(ve, s_mm, zs_done)
                i = ve.tensor_copy(out=rows_b[:, TG:2 * TG], in_=PE_[0:1, :])
                s_v.inc(i, 1)
            zsrow = s_v.n
            seq["pef_free"] = zsrow

            for q in range(NQ):
                @blk.sync
                def _(sy, q=q):
                    W(sy, s_mm, seq.get("dq_mm", 0))
                    for part in range(4):
                        i = sy.dma_start(
                            out=wdq_t[:, part * 11:(part + 1) * 11, :],
                            in_=wd_s[q, part * 11:(part + 1) * 11].rearrange("c p t -> p c t"))
                        s_wd.inc(i, 16)
                wq_ld = s_wd.n

                for do in range(4):
                    pbank = PC if do % 2 == 0 else PD

                    @blk.tensor
                    def _(te, q=q, do=do, pbank=pbank, wq_ld=wq_ld, zsrow=zsrow):
                        W(te, s_wd, wq_ld)
                        W(te, s_v, zsrow)
                        W(te, s_v, seq.get(f"dn_evac_{do % 2}", 0))
                        for hk in range(HC):
                            te.matmul(pbank[:], wdq_t[:, hk, do * 128:(do + 1) * 128],
                                      z_t[:, hk, :], start=(hk == 0), stop=False)
                        i = te.matmul(pbank[:], onesr_b[:], rows_b[:, TG:2 * TG],
                                      start=False, stop=True)
                        s_mm.inc(i, 1)
                    seq["dq_mm"] = s_mm.n
                    dn_mm = s_mm.n

                    @blk.vector
                    def _(ve, q=q, do=do, pbank=pbank, dn_mm=dn_mm):
                        W(ve, s_mm, dn_mm)
                        dc = q * 4 + do
                        i = ve.scalar_tensor_tensor(out=x_t[:, dc, :], in0=pbank[:],
                                                    scalar=sd_sb[:, dc:dc + 1], in1=x_t[:, dc, :],
                                                    op0=ALU.mult, op1=ALU.add)
                        s_v.inc(i, 1)
                    seq[f"dn_evac_{do % 2}"] = s_v.n
            ffn_done = s_v.n

            if store is None:
                store = True
            if not store:
                return

            @blk.sync
            def _(sy, k=k, store_to_y=store_to_y, ffn_done=ffn_done):
                W(sy, s_v, ffn_done)
                if store_to_y:
                    i = sy.dma_start(
                        out=y_out.rearrange("(c p) (k t) -> p c k t", p=128, t=TG)[:, :, k, :],
                        in_=x_t[:])
                    s_fin.inc(i, 16)
                else:
                    i = sy.dma_start(out=xbuf[k].rearrange("c p t -> p c t"), in_=x_t[:])
                    s_st.inc(i, 16)

        def emit_attn(blk, k, ln_done, store=True):
            for c in range(DC):
                par = c % 2
                gq = xn_t[:, c, par::2]

                @blk.tensor
                def _(te, c=c, gq=gq, ln_done=ln_done):
                    if c == 0:
                        W(te, s_v, ln_done)
                    else:
                        W(te, s_v, seq["att_done"])
                        W(te, s_a, seq["att_done_a"])
                        W(te, s_xc, seq.get("att_dump", 0))
                    te.matmul(PT[:, 0:128], gq[:, 0:128], ident_t[:], start=True, stop=True)
                    te.matmul(PT[:, 128:256], gq[:, 128:256], ident_t[:], start=True, stop=True)
                    te.matmul(PA[:, 0:256], gq[:, 0:128], gq, start=True, stop=True)
                    i = te.matmul(PA[:, 256:512], gq[:, 128:256], gq, start=True, stop=True)
                    s_mm.inc(i, 1)
                sc_mm = s_mm.n

                @blk.scalar
                def _(se, c=c, sc_mm=sc_mm):
                    W(se, s_mm, sc_mm)
                    se.copy(out=gtb_t[:, 0, :], in_=PT[:, 0:128])
                    i = se.copy(out=gtb_t[:, 1, :], in_=PT[:, 128:256])
                    s_a.inc(i, 1)
                gt_evac = s_a.n

                @blk.vector
                def _(ve, c=c, sc_mm=sc_mm):
                    W(ve, s_mm, sc_mm)
                    for qh in range(2):
                        ve.reduce_max(out=colmax[:, qh:qh + 1],
                                      in_=PA[:, qh * 256:(qh + 1) * 256], axis=AX.X)
                    i = ve.tensor_scalar(out=colmax[:, 2:4], in0=colmax[:, 0:2],
                                         scalar1=-INV_SQRT_HD, scalar2=None, op0=ALU.mult)
                    s_v.inc(i, 1)
                mx_done = s_v.n

                @blk.scalar
                def _(se, c=c, mx_done=mx_done):
                    W(se, s_v, mx_done)
                    for qh in range(2):
                        i = se.activation(out=att_t[:, qh, :],
                                          in_=PA[:, qh * 256:(qh + 1) * 256],
                                          func=AF.Exp, scale=c128_sb[:, 1:2],
                                          bias=colmax[:, 2 + qh:3 + qh])
                        s_a.inc(i, 1)
                exp_done = s_a.n

                @blk.vector
                def _(ve, c=c, exp_done=exp_done):
                    W(ve, s_a, exp_done)
                    for qh in range(2):
                        i = ve.reduce_sum(out=colsum[:, qh:qh + 1], in_=att_t[:, qh, :], axis=AX.X)
                    s_v.inc(i, 1)
                sums_done = s_v.n

                @blk.scalar
                def _(se, c=c, sums_done=sums_done):
                    W(se, s_v, sums_done)
                    se.activation(out=colsum[:, 2:4], in_=colsum[:, 0:2], func=AF.Ln)
                    i = se.activation(out=colsum[:, 2:4], in_=colsum[:, 2:4],
                                      func=AF.Exp, scale=c128_sb[:, 0:1])
                    s_a.inc(i, 1)
                rec_done = s_a.n

                @blk.vector
                def _(ve, c=c, rec_done=rec_done):
                    W(ve, s_a, rec_done)
                    W(ve, s_mm, seq.get("attT_mm", 0))
                    for qh in range(2):
                        i = ve.tensor_scalar(out=attb_t[:, qh, :], in0=att_t[:, qh, :],
                                             scalar1=colsum[:, 2 + qh:3 + qh], scalar2=None,
                                             op0=ALU.mult)
                    s_v.inc(i, 1)
                attn_norm = s_v.n

                @blk.tensor
                def _(te, c=c, attn_norm=attn_norm, gt_evac=gt_evac):
                    W(te, s_v, attn_norm)
                    W(te, s_a, gt_evac)
                    for qh in range(2):
                        for kh in range(2):
                            off = 256 + (qh * 2 + kh) * 128
                            i = te.matmul(PT[:, off:off + 128],
                                          attb_t[:, qh, kh * 128:(kh + 1) * 128], ident_t[:],
                                          start=True, stop=True)
                    s_mm.inc(i, 1)
                seq["attT_mm"] = s_mm.n
                attT_mm = s_mm.n

                @blk.scalar
                def _(se, c=c, attT_mm=attT_mm):
                    W(se, s_mm, attT_mm)
                    for t_i in range(4):
                        off = 256 + t_i * 128
                        i = se.copy(out=gtb_t[:, 2 + t_i, :], in_=PT[:, off:off + 128])
                    s_a.inc(i, 1)
                seq["att_done_a"] = s_a.n
                attT_evac = s_a.n

                @blk.tensor
                def _(te, c=c, gt_evac=gt_evac, attT_evac=attT_evac):
                    W(te, s_a, attT_evac)
                    for qh in range(2):
                        te.matmul(PB[:, qh * 128:(qh + 1) * 128], gtb_t[:, 0, :],
                                  gtb_t[:, 2 + qh * 2, :], start=True, stop=False)
                        i = te.matmul(PB[:, qh * 128:(qh + 1) * 128], gtb_t[:, 1, :],
                                      gtb_t[:, 2 + qh * 2 + 1, :], start=False, stop=True)
                    s_mm.inc(i, 1)
                out_mm = s_mm.n

                if dbg and k == 0 and c == 0 and dbg_ph == 3:
                    @blk.vector
                    def _(ve, out_mm=out_mm):
                        W(ve, s_mm, out_mm)
                        i = ve.tensor_copy(out=tscr[:, 0, 0:256], in_=PB[:, 0:256])
                        s_v.inc(i, 1)
                    dmp_v = s_v.n

                    @blk.gpsimd
                    def _(ge, dmp_v=dmp_v):
                        W(ge, s_v, dmp_v)
                        for dst, srct in [(dbg_att, att_t[:]), (dbg_attb, attb_t[:]),
                                          (dbg_gtb, gtb_t[:]),
                                          (dbg_pb, tscr[:, 0, 0:256]),
                                          (dbg_cm[:, 0:8], colmax[:])]:
                            i = ge.dma_start(out=dst, in_=srct)
                            s_g2.inc(i, 16)
                        i = ge.dma_start(out=dbg_cm[:, 8:16], in_=colsum[:])
                        s_g2.inc(i, 16)
                        W(ge, s_g2, s_g2.n)
                        i = ge.nop()
                        s_xc.inc(i, 1)
                    seq["att_dump"] = s_xc.n

                @blk.vector
                def _(ve, c=c, par=par, out_mm=out_mm):
                    W(ve, s_mm, out_mm)
                    if att_zero:
                        ve.memset(tscr[:, 0, 0:128], 0.0)
                    for qh in range(2):
                        src1 = tscr[:, 0, 0:128] if att_zero else PB[:, qh * 128:(qh + 1) * 128]
                        i = ve.tensor_tensor(
                            out=x_t[:, c, par::2][:, qh * 128:(qh + 1) * 128],
                            in0=x_t[:, c, par::2][:, qh * 128:(qh + 1) * 128],
                            in1=src1, op=ALU.add)
                    s_v.inc(i, 1)
                seq["att_done"] = s_v.n
            attn_done = s_v.n
            seq["pa_free"] = seq["att_done_a"]
            seq["pb_free"] = seq["att_done"]
            seq["xn_pe_free"] = s_mm.n

            if not store:
                return

            @blk.sync
            def _(sy, k=k, attn_done=attn_done):
                W(sy, s_v, attn_done)
                i = sy.dma_start(out=xbuf[k].rearrange("c p t -> p c t"), in_=x_t[:])
                s_st.inc(i, 16)

        def emit_dbg_x2():
            with nc.Block() as blkd:
                @blkd.gpsimd
                def _(ge):
                    i = ge.dma_start(out=dbg_x2, in_=x_t[:])
                    s_g2.inc(i, 16)
                    i = ge.dma_start(out=dbg_z, in_=z_t[:])
                    s_g2.inc(i, 16)
                    W(ge, s_g2, s_g2.n)

        with nc.Block() as blk:
            @blk.sync
            def _(sy):
                W(sy, s_cv, CONV_DONE)
                W(sy, s_gd, PH1_GD)

        for k in range(nk):
            if sub == (2, 3, 4):
                ld = emit_ld_shared(k)
                with nc.Block() as blk:
                    ln2 = emit_ln([ld], blk=blk)
                    emit_ffn(blk, k, store_to_y=False, store=False, ln_done=ln2)
                    if trunc >= 3:
                        rc = emit_recast(blk)
                        ln3 = emit_ln([rc], blk=blk)
                        emit_attn(blk, k, ln3, store=False)
                    if trunc >= 4:
                        rc = emit_recast(blk)
                        ln4 = emit_ln([rc], blk=blk)
                        emit_ffn(blk, k, store_to_y=True, store=True, ln_done=ln4)
                continue
            stx = None
            if 9 in sub:
                ld = emit_ld_shared(k)
                emit_ln(ld)
                if dbg and k == 0:
                    emit_dbg_x2()
            if 2 in sub:
                ld = emit_ld_shared(k)
                emit_ln(ld)
                with nc.Block() as blk:
                    emit_ffn(blk, k, store_to_y=False)
                stx = s_st.n
                if dbg and k == 0 and dbg_ph == 2:
                    emit_dbg_x2()
            if 3 in sub:
                ld = emit_ld_xbuf(k, stx) if stx else emit_ld_shared(k)
                ln3 = emit_ln(ld)
                if dbg and k == 0 and dbg_ph == 3:
                    with nc.Block() as blkq:
                        @blkq.gpsimd
                        def _(ge):
                            i = ge.dma_start(out=dbg_x, in_=x_t[:])
                            s_g2.inc(i, 16)
                            i = ge.dma_start(out=dbg_xn, in_=xn_t[:])
                            s_g2.inc(i, 16)
                            W(ge, s_g2, s_g2.n)
                with nc.Block() as blk:
                    emit_attn(blk, k, ln3)
                stx = s_st.n
                if dbg and k == 0 and dbg_ph == 3:
                    emit_dbg_x2()
            if 4 in sub:
                ld = emit_ld_xbuf(k, stx) if stx else emit_ld_shared(k)
                emit_ln(ld)
                with nc.Block() as blk:
                    emit_ffn(blk, k, store_to_y=True)


    with nc.Block() as blk:
        @blk.gpsimd
        def _(ge):
            if do_rest:
                ge.wait_ge(s_fin.h, s_fin.n)
            else:
                ge.wait_ge(s_gd.h, s_gd.n)

    return nc


_NC_CACHE = None


def prepare_in_maps(L0_attn_a_input, L0_attn_a_W_dt_in, L0_attn_a_W_dt_out, L0_attn_a_b_dt_out,
                    L0_attn_a_W_B, L0_attn_a_W_C, L0_attn_a_D, L0_attn_a_A_log,
                    L0_ffn_wq_gate, L0_ffn_s_gate, L0_ffn_wq_up, L0_ffn_s_up,
                    L0_ffn_wq_down, L0_ffn_s_down):
    x = np.asarray(L0_attn_a_input)

    def chunkcol(v):
        v = np.asarray(v)
        n = v.shape[0] // 128
        return np.ascontiguousarray(v.reshape(n, 128).T.astype(np.float32))

    perms = {}
    for j in (0, 1):
        own = np.arange(1024) + 1024 * j
        oth = np.arange(1024) + 1024 * (1 - j)
        perms[j] = np.concatenate([own, oth])

    wqg = np.asarray(L0_ffn_wq_gate)
    wqu = np.asarray(L0_ffn_wq_up)
    wqd = np.asarray(L0_ffn_wq_down)
    wdtin = np.asarray(L0_attn_a_W_dt_in)
    wdtout = np.asarray(L0_attn_a_W_dt_out)
    wb = np.asarray(L0_attn_a_W_B)
    wc = np.asarray(L0_attn_a_W_C)

    per_role = {}
    for j in (0, 1):
        p = perms[j]
        own = p[0:1024]
        per_role[j] = {
            "wg_i": np.ascontiguousarray(wqg.T[p, :]),
            "wu_i": np.ascontiguousarray(wqu.T[p, :]),
            "wd_i": np.ascontiguousarray(wqd.T[:, p]),
            "wdtin": np.ascontiguousarray(wdtin.T[p, :].astype(np.float32)),
            "wdtout": np.ascontiguousarray(wdtout.T[:, own].astype(np.float32)),
            "wbc": np.ascontiguousarray(
                np.concatenate([wb, wc], axis=0).T[p, :].astype(np.float32)),
            "alog": np.ascontiguousarray(
                np.asarray(L0_attn_a_A_log)[own, :].reshape(8, 128, NS)
                .transpose(1, 0, 2).reshape(128, 8 * NS).astype(np.float32)),
            "dvec": np.ascontiguousarray(
                np.asarray(L0_attn_a_D)[own].reshape(8, 128).T.astype(np.float32)),
            "bdt": np.ascontiguousarray(
                np.asarray(L0_attn_a_b_dt_out)[own].reshape(8, 128).T.astype(np.float32)),
            "sd_in": np.ascontiguousarray(
                np.asarray(L0_ffn_s_down)[p].reshape(16, 128).T.astype(np.float32)),
            "role_in": np.array([[j]], dtype=np.int32),
        }

    indic = np.zeros((32, 32 * 128), dtype=np.float32)
    for n in range(32):
        indic[n, n * 128:(n + 1) * 128] = 1.0

    shared_in = {
        "sg_in": chunkcol(L0_ffn_s_gate),
        "su_in": chunkcol(L0_ffn_s_up),
        "ident_in": np.eye(128).astype(BF),
        "indic_in": indic.astype(BF),
        "ones_b_in": np.ones((128, 1)).astype(BF),
        "onesr_f_in": np.ones((1, 128), np.float32),
        "onesr_b_in": np.ones((1, 128)).astype(BF),
        "negr_b_in": np.full((1, 128), -128.0).astype(BF),
        "nones_b_in": np.full((128, 1), -128.0).astype(BF),
        "c128_in": np.ascontiguousarray(np.stack([np.full(128, -1.0), np.full(128, INV_SQRT_HD)], axis=1).astype(np.float32)),
        "c1_in": np.array([[EPS, 1.0, -0.5, -1.0, 2.0, 4.0, 6.0, 8.0]], dtype=np.float32),
    }

    in_maps = []
    for c in range(8):
        b = c // 2
        j = c % 2
        m = dict(shared_in)
        m.update(per_role[j])
        m["x_in"] = np.ascontiguousarray(x[b].T[perms[j], :].astype(np.float32))
        in_maps.append(m)
    return in_maps, perms


def assemble_output(results, perms):
    out = np.empty((B, T, D), dtype=np.float32)
    for c in range(8):
        b = c // 2
        j = c % 2
        y = np.asarray(results[c]["y_out"])
        inv = np.argsort(perms[j])
        y = y[inv, :]
        for k in range(4):
            g = j + 2 * k
            out[b, g * TG:(g + 1) * TG, :] = y[:, k * TG:(k + 1) * TG].T
    return out


def kernel(**inputs):
    global _NC_CACHE
    in_maps, perms = prepare_in_maps(**inputs)
    if _NC_CACHE is None:
        _NC_CACHE = build_program()
    nc = _NC_CACHE

    res = run_bass_kernel_spmd(nc, in_maps, core_ids=list(range(8)))
    return assemble_output(res.results, perms)
